# version 21
# baseline (speedup 1.0000x reference)
import numpy as np
import ml_dtypes
import concourse.bass as bass
import concourse.mybir as mybir
from concourse.bass_utils import run_bass_kernel_spmd

BF16NP = ml_dtypes.bfloat16
F32 = mybir.dt.float32
BF16 = mybir.dt.bfloat16
AF = mybir.ActivationFunctionType
ALU = mybir.AluOpType

P = 128
C = 1024
NCH = 2
TPC = NCH * C
ND = 8
NH = 32
INV_D = 1.0 / 1024.0
EPS = 1e-5

ENGINES = ("sync", "gpsimd", "tensor", "vector", "scalar")


class Sem:
    __slots__ = ("h", "n")

    def __init__(self, h):
        self.h = h
        self.n = 0


class Plan:
    def __init__(self):
        self.ops = {e: [] for e in ENGINES}

    def add(self, eng, fn, waits=(), inc=None, inc_n=1):
        w = tuple((s, v) for w_ in waits if w_ is not None
                  for (s, v) in (w_,) if s is not None and v > 0)
        self.ops[eng].append((fn, w, inc, inc_n))
        if inc is not None:
            inc.n += inc_n
            return (inc, inc.n)
        return None


def build_nc():
    nc = bass.Bass("TRN2", target_bir_lowering=False)

    xT = nc.dram_tensor("xT", [1024, TPC], F32, kind="ExternalInput").ap()
    wT = nc.dram_tensor("wT", [2, P, 4096], F32, kind="ExternalInput").ap()
    w1T = nc.dram_tensor("w1T", [8, P, 4096], F32, kind="ExternalInput").ap()
    w2T = nc.dram_tensor("w2T", [8, P, 4096], F32, kind="ExternalInput").ap()
    h1T = nc.dram_tensor("h1T", [32, P, 4096], F32, kind="ExternalInput").ap()
    h2T = nc.dram_tensor("h2T", [32, P, 4096], F32, kind="ExternalInput").ap()
    w3T = nc.dram_tensor("w3T", [8, P, 4096], F32, kind="ExternalInput").ap()
    s1c = nc.dram_tensor("s1c", [P, 32], F32, kind="ExternalInput").ap()
    s2c = nc.dram_tensor("s2c", [P, 32], F32, kind="ExternalInput").ap()
    sh1c = nc.dram_tensor("sh1c", [P, 32], F32, kind="ExternalInput").ap()
    sh2c = nc.dram_tensor("sh2c", [P, 32], F32, kind="ExternalInput").ap()
    s3c = nc.dram_tensor("s3c", [P, 8], F32, kind="ExternalInput").ap()
    biasc = nc.dram_tensor("biasc", [P, 8], F32, kind="ExternalInput").ap()
    ones_d = nc.dram_tensor("ones", [P, P], BF16, kind="ExternalInput").ap()
    triu_d = nc.dram_tensor("triu", [P, P], BF16, kind="ExternalInput").ap()
    outT = nc.dram_tensor("outT", [1024, TPC], F32, kind="ExternalOutput").ap()

    plan = Plan()

    from contextlib import ExitStack
    with ExitStack() as stack:
        ec = stack.enter_context
        x_sb = ec(nc.sbuf_tensor("x_sb", [P, ND * C], F32))
        u_sb = ec(nc.sbuf_tensor("u_sb", [P, NH * C], BF16))
        v_sb = ec(nc.sbuf_tensor("v_sb", [P, NH * C], BF16))
        wb = ec(nc.sbuf_tensor("wb", [P, 4 * 4096], BF16))
        stat = ec(nc.sbuf_tensor("stat", [P, 3 * C], F32))
        ones_sb = ec(nc.sbuf_tensor("ones_sb", [P, P], BF16))
        triu_sb = ec(nc.sbuf_tensor("triu_sb", [P, P], BF16))
        s1_sb = ec(nc.sbuf_tensor("s1_sb", [P, 32], F32))
        s2_sb = ec(nc.sbuf_tensor("s2_sb", [P, 32], F32))
        sh1_sb = ec(nc.sbuf_tensor("sh1_sb", [P, 32], F32))
        sh2_sb = ec(nc.sbuf_tensor("sh2_sb", [P, 32], F32))
        s3_sb = ec(nc.sbuf_tensor("s3_sb", [P, 8], F32))
        bias_sb = ec(nc.sbuf_tensor("bias_sb", [P, 8], F32))
        cst = ec(nc.sbuf_tensor("cst", [P, 2], F32))
        ps0 = ec(nc.psum_tensor("ps0", [P, 1024], F32))
        ps1 = ec(nc.psum_tensor("ps1", [P, 1024], F32))
        ps2 = ec(nc.psum_tensor("ps2", [P, 1024], F32))
        ps3 = ec(nc.psum_tensor("ps3", [P, 1024], F32))
        h_sy = ec(nc.semaphore("s_sy"))
        h_cst = ec(nc.semaphore("s_cst"))
        h_xl = [ec(nc.semaphore("s_xl%d" % i)) for i in range(ND)]
        h_st = [ec(nc.semaphore("s_st%d" % i)) for i in range(ND)]
        h_ex = [ec(nc.semaphore("s_ex%d" % i)) for i in range(2)]
        h_w = [ec(nc.semaphore("s_w%d" % q)) for q in range(4)]
        h_pe = ec(nc.semaphore("s_pe"))
        h_sc = ec(nc.semaphore("s_sc"))
        h_ve = ec(nc.semaphore("s_ve"))
        block = ec(nc.Block())
        ps = [ps0, ps1, ps2, ps3]
        s_sy, s_pe, s_sc, s_ve = Sem(h_sy), Sem(h_pe), Sem(h_sc), Sem(h_ve)
        s_cst = Sem(h_cst)
        s_w = [Sem(h) for h in h_w]
        s_xl = [Sem(h) for h in h_xl]
        s_st = [Sem(h) for h in h_st]
        s_ex = [Sem(h) for h in h_ex]

        XN, XBF, X2, HA = 0, ND * C, 2 * ND * C, 3 * ND * C

        def x_t(i):
            return x_sb[:, i * C:(i + 1) * C]

        def x_th(i, h):
            return x_sb[:, i * C + h * 512: i * C + (h + 1) * 512]

        def vre(off, i):
            return v_sb[:, off + i * C: off + (i + 1) * C]

        def vre_h(off, i, h):
            return v_sb[:, off + i * C + h * 512: off + i * C + (h + 1) * 512]

        def u_t(o):
            return u_sb[:, o * C:(o + 1) * C]

        def u_th(i, h):
            return u_sb[:, i * C + h * 512: i * C + (h + 1) * 512]

        def v_t(o):
            return v_sb[:, o * C:(o + 1) * C]

        def v_th(i, h):
            return v_sb[:, i * C + h * 512: i * C + (h + 1) * 512]

        ps_free = [None, None, None, None]
        ps_seq = [0, 0, 0, 0]
        ps_ctr = [0]

        def psum_acquire(pool=(0, 1, 2, 3)):
            k = min(pool, key=lambda j: ps_seq[j])
            ps_ctr[0] += 1
            ps_seq[k] = ps_ctr[0]
            return k, ps_free[k]

        dma_fns = []
        dma_ready = []
        dma_mm_done = []
        dma_planned = [0]

        def enqueue_wdma(dram, n_i, ncols, og):
            def mk(q, dram=dram, og=og):
                def fn(e):
                    dst = wb[:, q * 4096: q * 4096 + 4096]
                    return e.dma_start(out=dst, in_=dram[og])
                return fn
            idx = len(dma_fns)
            dma_fns.append(mk)
            dma_ready.append(None)
            dma_mm_done.append(None)
            return idx

        def pump_dmas():
            while dma_planned[0] < len(dma_fns):
                i = dma_planned[0]
                if i >= 4 and dma_mm_done[i - 4] is None:
                    break
                q = i % 4
                waits = [] if i < 4 else [dma_mm_done[i - 4]]
                ev = plan.add("gpsimd", dma_fns[i](q), waits=waits, inc=s_w[q], inc_n=16)
                dma_ready[i] = ev
                dma_planned[0] += 1

        def slot_of(idx):
            return idx % 4

        def mm_group(k, lhs_aps, rhs_fn, first_waits, per_i_waits=None):
            n = len(lhs_aps)
            ev = None
            for i in range(n):
                for h in range(2):
                    waits = []
                    if i == 0 and h == 0:
                        waits += list(first_waits)
                    if per_i_waits is not None and h == 0:
                        w = per_i_waits(i)
                        if w:
                            waits += w
                    lhs = lhs_aps[i]
                    rhs = rhs_fn(i, h)
                    out = ps[k][:, h * 512:(h + 1) * 512]
                    is_last = (i == n - 1 and h == 1)

                    def fn(e, out=out, lhs=lhs, rhs=rhs, st=(i == 0), sp=(i == n - 1)):
                        return e.matmul(out, lhs, rhs, start=st, stop=sp)
                    if is_last:
                        ev = plan.add("tensor", fn, waits=waits, inc=s_pe)
                    else:
                        plan.add("tensor", fn, waits=waits)
            return ev

        def dma_to(dst, src):
            def fn(e, dst=dst, src=src):
                return e.dma_start(out=dst, in_=src)
            return fn

        consts_evt = None
        for dst, src in (
            (ones_sb[:, :], ones_d), (triu_sb[:, :], triu_d),
            (s1_sb[:, :], s1c), (s2_sb[:, :], s2c),
            (sh1_sb[:, :], sh1c), (sh2_sb[:, :], sh2c),
            (s3_sb[:, :], s3c), (bias_sb[:, :], biasc),
        ):
            consts_evt = plan.add("scalar", dma_to(dst, src), inc=s_cst, inc_n=16)

        plan.add("vector", lambda e: e.memset(cst[:, 0:1], EPS))
        cst_evt = plan.add("vector", lambda e: e.memset(cst[:, 1:2], -0.5), inc=s_ve)

        stat_m = stat[:, 0:C]
        stat_msq = stat[:, C:2 * C]
        stat_r = stat[:, 2 * C:3 * C]

        def ln_phase(x_ready, war_xbf_x2, war_xn, war_tmp, war_ha,
                     pool=(0, 1, 2, 3), make_xn=True, precast=None):
            if precast is not None:
                xbf_evt, x2_evt = precast
            else:
                xbf_evt = [None] * ND
                x2_evt = [None] * ND
                for i in range(ND):
                    def fc(e, i=i):
                        return e.tensor_copy(vre(XBF, i), x_t(i))
                    xbf_evt[i] = plan.add("vector", fc,
                                          waits=x_ready(i) + [war_xbf_x2], inc=s_ve)
                for i in range(ND):
                    def fs(e, i=i):
                        return e.activation(vre(X2, i), x_t(i), AF.Square)
                    x2_evt[i] = plan.add("scalar", fs,
                                         waits=x_ready(i) + [war_xbf_x2], inc=s_sc)

            kS, freeS = psum_acquire(pool)
            evS = mm_group(kS, [ones_sb[:, 0:P]] * ND,
                           lambda i, h: vre_h(XBF, i, h),
                           first_waits=[freeS, consts_evt],
                           per_i_waits=lambda i: [xbf_evt[i]])
            kSS, freeSS = psum_acquire(pool)
            evSS = mm_group(kSS, [ones_sb[:, 0:P]] * ND,
                            lambda i, h: vre_h(X2, i, h),
                            first_waits=[freeSS, consts_evt],
                            per_i_waits=lambda i: [x2_evt[i]])

            m_ap, msq_ap, r_ap = stat_m, stat_msq, stat_r
            mbf_ap = vre(HA, 0)
            rbf_ap = vre(HA, 1)

            ev_m = plan.add("vector",
                            lambda e, kS=kS: e.tensor_scalar_mul(m_ap, ps[kS][:, 0:C], INV_D),
                            waits=[evS, (s_sc, s_sc.n)], inc=s_ve)
            ps_free[kS] = ev_m
            ev_msq = plan.add("scalar",
                              lambda e: e.activation(msq_ap, m_ap, AF.Square),
                              waits=[ev_m], inc=s_sc)
            mbf_evt = plan.add("vector", lambda e: e.tensor_copy(mbf_ap, m_ap),
                     waits=[war_ha, ev_m], inc=s_ve)
            sub_evt = [None] * ND
            n_pre = ND if not make_xn else 3

            def emit_subs(rng):
                for i in rng:
                    def fsub(e, i=i):
                        return e.tensor_sub(u_t(i), vre(XBF, i), mbf_ap)
                    sub_evt[i] = plan.add("vector", fsub,
                                          waits=[war_tmp, mbf_evt], inc=s_ve)

            emit_subs(range(n_pre))
            ev_var = plan.add("vector",
                              lambda e, kSS=kSS: e.scalar_tensor_tensor(
                                  r_ap, ps[kSS][:, 0:C], INV_D, msq_ap,
                                  ALU.mult, ALU.subtract),
                              waits=[evSS, ev_msq], inc=s_ve)
            ps_free[kSS] = ev_var
            ln_evt = plan.add("scalar",
                              lambda e: e.activation(r_ap, r_ap, AF.Ln, bias=cst[:, 0:1]),
                              waits=[ev_var, cst_evt], inc=s_sc)
            rstd_evt = plan.add("scalar",
                                lambda e: e.activation(r_ap, r_ap, AF.Exp, scale=cst[:, 1:2]),
                                waits=[ln_evt], inc=s_sc)
            emit_subs(range(n_pre, ND))
            if not make_xn:
                return sub_evt, rstd_evt, ev_msq
            rbf_evt = plan.add("vector", lambda e: e.tensor_copy(rbf_ap, r_ap),
                                waits=[rstd_evt], inc=s_ve)
            xn_evt = [None] * ND
            for i in range(ND):
                def fmul(e, i=i):
                    return e.tensor_mul(vre(XN, i), u_t(i), rbf_ap)
                xn_evt[i] = plan.add("vector", fmul,
                                     waits=[war_xn, rbf_evt, sub_evt[i]], inc=s_ve)
            return xn_evt, rstd_evt, ev_msq

        store_done = None
        war_u = None
        war_ha_ve = None
        attn_end_pe = None
        w1w2_end_pe = None
        h1_end_pe = None
        h2_end_pe = None
        u_all_evt = None

        next_pre = None
        for cidx in range(NCH):
            c0 = cidx * C

            if next_pre is None:
                xl_evt = [None] * ND
                for i in range(ND):
                    def fxl(e, i=i, c0=c0):
                        return e.dma_start(out=x_t(i),
                                           in_=xT[i * P:(i + 1) * P, c0:c0 + C])
                    xl_evt[i] = plan.add("sync", fxl, waits=[],
                                         inc=s_xl[i], inc_n=16)
                pre_cast = None
            else:
                xl_evt = next_pre["xl"]
                pre_cast = (next_pre["xbf"], next_pre["x2"])
                next_pre = None

            attn_dma = [enqueue_wdma(wT, ND, 512, og) for og in range(2)]
            w12_dma = []
            for og in range(8):
                w12_dma.append((enqueue_wdma(w1T, ND, 512, og),
                                enqueue_wdma(w2T, ND, 512, og)))
            h1_dma = [enqueue_wdma(h1T, NH, 128, o) for o in range(NH)]
            h2_dma = [enqueue_wdma(h2T, NH, 128, o) for o in range(NH)]
            w3_dma = [enqueue_wdma(w3T, NH, 128, o) for o in range(ND)]
            pump_dmas()

            sub_evt, rstd1_evt, msq1_evt = ln_phase(
                lambda i: [xl_evt[i]],
                war_xbf_x2=h2_end_pe, war_xn=w1w2_end_pe,
                war_tmp=war_u, war_ha=war_u, pool=(2, 3), make_xn=False,
                precast=pre_cast)

            ha_evt = [None] * ND
            xadd_evt = [None] * ND
            tt_evt = [None] * ND
            xbf2_evt = [None] * ND
            x22_evt = [None] * ND
            sbf_state = [None]

            def emit_attn(o):
                og = o // 4
                j = o % 4
                idx = attn_dma[og]
                q = slot_of(idx)
                blocks = [wb[:, q * 4096 + i * 512 + j * P: q * 4096 + i * 512 + j * P + P]
                          for i in range(ND)]
                kH, freeH = psum_acquire()
                evH = mm_group(kH, blocks, lambda i, h: u_th(i, h),
                               first_waits=[freeH, dma_ready[idx]],
                               per_i_waits=lambda i: [sub_evt[i]])
                if j == 3:
                    dma_mm_done[idx] = (s_pe, s_pe.n)
                    pump_dmas()

                scr = stat_m if o % 2 == 0 else stat_msq
                scr_war = [msq1_evt] if o < 2 else [ha_evt[o - 2]]
                def ftt(e, kH=kH, scr=scr):
                    return e.tensor_tensor(scr, ps[kH][:, 0:C], stat_r, ALU.mult)
                tt_evt[o] = plan.add("vector", ftt,
                                     waits=[evH, rstd1_evt] + scr_war, inc=s_ve)
                ps_free[kH] = tt_evt[o]

                ha_war = [war_ha_ve] if (war_ha_ve and o < 2) else []
                def fha(e, o=o, scr=scr):
                    return e.activation(vre(HA, o), scr, AF.Identity,
                                        bias=bias_sb[:, o:o + 1])
                ha_evt[o] = plan.add("scalar", fha,
                                     waits=[tt_evt[o], consts_evt] + ha_war, inc=s_sc)

            def emit_y(o):
                kY, freeY = psum_acquire()
                evY = None
                for hh in range(2):
                    waits = [freeY, consts_evt, ha_evt[o]] if hh == 0 else []
                    def f1(e, kY=kY, o=o, hh=hh):
                        return e.matmul(ps[kY][:, hh * 512:(hh + 1) * 512],
                                        triu_sb[:, 0:P], vre_h(HA, o, hh),
                                        start=True, stop=(o == 0))
                    if o == 0 and hh == 1:
                        evY = plan.add("tensor", f1, waits=waits, inc=s_pe)
                    else:
                        plan.add("tensor", f1, waits=waits)
                if o > 0:
                    for hh in range(2):
                        waits = [sbf_state[0]] if hh == 0 else []
                        def f2(e, kY=kY, hh=hh):
                            return e.matmul(ps[kY][:, hh * 512:(hh + 1) * 512],
                                            ones_sb[:, 0:P],
                                            v_sb[:, XN + hh * 512: XN + (hh + 1) * 512],
                                            start=False, stop=True)
                        if hh == 1:
                            evY = plan.add("tensor", f2, waits=waits, inc=s_pe)
                        else:
                            plan.add("tensor", f2, waits=waits)
                def fadd(e, o=o, kY=kY):
                    return e.tensor_add(x_t(o), x_t(o), ps[kY][:, 0:C])
                xadd_evt[o] = plan.add("vector", fadd,
                                       waits=[evY, xl_evt[o]], inc=s_ve)
                ps_free[kY] = xadd_evt[o]

                if o == 0:
                    sbf_state[0] = plan.add(
                        "vector",
                        lambda e: e.tensor_copy(vre(XN, 0), vre(HA, 0)),
                        waits=[ha_evt[0]] + ([w1w2_end_pe] if w1w2_end_pe else []),
                        inc=s_ve)
                elif o < ND - 1:
                    def fsu(e, o=o):
                        return e.tensor_add(vre(XN, 0), vre(XN, 0), vre(HA, o))
                    sbf_state[0] = plan.add("vector", fsu,
                                            waits=[ha_evt[o], evY, sbf_state[0]],
                                            inc=s_ve)

                def fc2(e, o=o):
                    return e.tensor_copy(vre(XBF, o), x_t(o))
                xbf2_evt[o] = plan.add("vector", fc2, waits=[xadd_evt[o]],
                                       inc=s_ve)
                def fs2(e, o=o):
                    return e.activation(vre(X2, o), x_t(o), AF.Square)
                x22_evt[o] = plan.add("scalar", fs2, waits=[xadd_evt[o]],
                                      inc=s_sc)

            for o in range(ND):
                emit_attn(o)
                if o >= 1:
                    emit_y(o - 1)
            emit_y(ND - 1)
            attn_end_pe = (s_pe, s_pe.n)

            xn2_evt, rstd2_evt, msq2_evt = ln_phase(lambda i: [xadd_evt[i]],
                                     war_xbf_x2=attn_end_pe,
                                     war_xn=attn_end_pe,
                                     war_tmp=attn_end_pe, war_ha=attn_end_pe,
                                     precast=(xbf2_evt, x22_evt))
            ln2_stats_end = (s_pe, s_pe.n)
            ln2_ve_end = (s_ve, s_ve.n)
            ln2_sc_end = (s_sc, s_sc.n)

            u_evt = [None] * NH
            for og in range(8):
                idx1, idx2 = w12_dma[og]
                q1, q2 = slot_of(idx1), slot_of(idx2)
                for j in range(4):
                    o = og * 4 + j
                    b1 = [wb[:, q1 * 4096 + i * 512 + j * P: q1 * 4096 + i * 512 + j * P + P]
                          for i in range(ND)]
                    b2 = [wb[:, q2 * 4096 + i * 512 + j * P: q2 * 4096 + i * 512 + j * P + P]
                          for i in range(ND)]
                    kA, freeA = psum_acquire()
                    evA = mm_group(kA, b1, lambda i, h: vre_h(XN, i, h),
                                   first_waits=[freeA, dma_ready[idx1]],
                                   per_i_waits=lambda i: [xn2_evt[i]])
                    kB, freeB = psum_acquire()
                    evB = mm_group(kB, b2, lambda i, h: vre_h(XN, i, h),
                                   first_waits=[freeB, dma_ready[idx2]])

                    at = o % 2
                    if o >= 2:
                        a_war = [u_evt[o - 2]]
                    else:
                        a_war = [attn_end_pe, war_ha_ve]

                    def fsilu(e, at=at, kA=kA, o=o):
                        return e.activation(vre(HA, at), ps[kA][:, 0:C], AF.Silu,
                                            scale=s1_sb[:, o:o + 1])
                    a_evt = plan.add("scalar", fsilu, waits=[evA] + a_war, inc=s_sc)
                    ps_free[kA] = a_evt

                    u_war = [war_u] if (war_u and o < 1) else []
                    def fu(e, o=o, kB=kB, at=at):
                        return e.scalar_tensor_tensor(u_t(o), ps[kB][:, 0:C],
                                                      s2_sb[:, o:o + 1], vre(HA, at),
                                                      ALU.mult, ALU.mult)
                    u_evt[o] = plan.add("vector", fu, waits=[evB, a_evt] + u_war, inc=s_ve)
                    ps_free[kB] = u_evt[o]
                dma_mm_done[idx1] = dma_mm_done[idx2] = (s_pe, s_pe.n)
                pump_dmas()
            w1w2_end_pe = (s_pe, s_pe.n)
            u_all_evt = u_evt[NH - 1]
            war_ha_ve = u_evt[NH - 1]

            v_evt = [None] * NH
            for o in range(NH):
                idx = h1_dma[o]
                q = slot_of(idx)
                blocks = [wb[:, q * 4096 + i * P: q * 4096 + (i + 1) * P]
                          for i in range(NH)]
                kP, freeP = psum_acquire()
                ev = mm_group(kP, blocks, lambda i, h: u_th(i, h),
                              first_waits=[freeP, dma_ready[idx]],
                              per_i_waits=lambda i: [u_evt[i]])
                dma_mm_done[idx] = (s_pe, s_pe.n)
                pump_dmas()
                v_war = [w1w2_end_pe, war_ha_ve] if o < 1 else []
                def fg(e, o=o, kP=kP):
                    return e.activation(v_t(o), ps[kP][:, 0:C], AF.Gelu,
                                        scale=sh1_sb[:, o:o + 1])
                v_evt[o] = plan.add("scalar", fg, waits=[ev] + v_war, inc=s_sc)
                ps_free[kP] = v_evt[o]
            h1_end_pe = (s_pe, s_pe.n)

            w_evt = [None] * NH
            for o in range(NH):
                idx = h2_dma[o]
                q = slot_of(idx)
                blocks = [wb[:, q * 4096 + i * P: q * 4096 + (i + 1) * P]
                          for i in range(NH)]
                kP, freeP = psum_acquire()
                ev = mm_group(kP, blocks, lambda i, h: v_th(i, h),
                              first_waits=[freeP, dma_ready[idx]],
                              per_i_waits=lambda i: [v_evt[i]])
                dma_mm_done[idx] = (s_pe, s_pe.n)
                pump_dmas()
                w_war = [h1_end_pe] if o < 1 else []
                def fg2(e, o=o, kP=kP):
                    return e.activation(u_t(o), ps[kP][:, 0:C], AF.Gelu,
                                        scale=sh2_sb[:, o:o + 1])
                w_evt[o] = plan.add("scalar", fg2, waits=[ev] + w_war, inc=s_sc)
                ps_free[kP] = w_evt[o]
            h2_end_pe = (s_pe, s_pe.n)

            xout_evt = [None] * ND
            st_evt = [None] * ND
            ex_evt = [None, None]
            if cidx + 1 < NCH:
                next_pre = {"xl": [None] * ND, "xbf": [None] * ND,
                            "x2": [None] * ND}
                c1e = (cidx + 1) * C
                for t_, dst_ in ((6, stat_m), (7, stat_msq)):
                    def fex(e, t_=t_, dst_=dst_, c1e=c1e):
                        return e.dma_start(out=dst_,
                                           in_=xT[t_ * P:(t_ + 1) * P, c1e:c1e + C])
                    ex_evt[t_ - 6] = plan.add("scalar", fex,
                                              waits=[ln2_ve_end, ln2_sc_end],
                                              inc=s_ex[t_ - 6], inc_n=16)
            for o in range(ND):
                idx = w3_dma[o]
                q = slot_of(idx)
                blocks = [wb[:, q * 4096 + i * P: q * 4096 + (i + 1) * P]
                          for i in range(NH)]
                kP, freeP = psum_acquire(pool=(0, 1))
                ev = mm_group(kP, blocks, lambda i, h: u_th(i, h),
                              first_waits=[freeP, dma_ready[idx]],
                              per_i_waits=lambda i: [w_evt[i]])
                dma_mm_done[idx] = (s_pe, s_pe.n)
                pump_dmas()
                def fw3(e, o=o, kP=kP):
                    return e.scalar_tensor_tensor(x_t(o), ps[kP][:, 0:C],
                                                  s3_sb[:, o:o + 1], x_t(o),
                                                  ALU.mult, ALU.add)
                xout_evt[o] = plan.add("vector", fw3, waits=[ev], inc=s_ve)
                ps_free[kP] = xout_evt[o]

                def fst(e, o=o, c0=c0):
                    return e.dma_start(out=outT[o * P:(o + 1) * P, c0:c0 + C],
                                       in_=x_t(o))
                st_evt[o] = plan.add("sync", fst, waits=[xout_evt[o]],
                                     inc=s_st[o], inc_n=16)

                if cidx + 1 < NCH:
                    c1 = (cidx + 1) * C
                    def fxl2(e, o=o, c1=c1):
                        return e.dma_start(out=x_t(o),
                                           in_=xT[o * P:(o + 1) * P, c1:c1 + C])
                    next_pre["xl"][o] = plan.add("sync", fxl2,
                                                 waits=[st_evt[o]],
                                                 inc=s_xl[o], inc_n=16)
                    xsrc = x_t(o) if o < 6 else (stat_m if o == 6 else stat_msq)
                    src_rdy = ([next_pre["xl"][o]] if o < 6 else [ex_evt[o - 6]])
                    def fc3(e, o=o, xsrc=xsrc):
                        return e.tensor_copy(vre(XBF, o), xsrc)
                    next_pre["xbf"][o] = plan.add(
                        "vector", fc3,
                        waits=src_rdy + [ln2_stats_end], inc=s_ve)
                    def fs3(e, o=o, xsrc=xsrc):
                        return e.activation(vre(X2, o), xsrc, AF.Square)
                    next_pre["x2"][o] = plan.add(
                        "scalar", fs3,
                        waits=src_rdy + [ln2_stats_end], inc=s_sc)
            war_u = (s_pe, s_pe.n)
            store_done = list(st_evt)

        plan.add("sync", None, waits=list(store_done))

        def emitter(eng_name):
            def run(e):
                waited = {}
                for fn, waits, inc, inc_n in plan.ops[eng_name]:
                    for (s, v) in waits:
                        key = id(s)
                        if waited.get(key, 0) < v:
                            e.wait_ge(s.h, v)
                            waited[key] = v
                    if fn is None:
                        continue
                    inst = fn(e)
                    if inc is not None:
                        inst.then_inc(inc.h, inc_n)
            return run

        block.sync(emitter("sync"))
        block.gpsimd(emitter("gpsimd"))
        block.tensor(emitter("tensor"))
        block.vector(emitter("vector"))
        block.scalar(emitter("scalar"))

    return nc


_NC_CACHE = None


def kernel(L0_attn_x, L0_attn_W, L0_attn_bias,
           L0_ffn_w1_w, L0_ffn_w1_s, L0_ffn_w2_w, L0_ffn_w2_s,
           L0_ffn_h1_w, L0_ffn_h1_s, L0_ffn_h2_w, L0_ffn_h2_s,
           L0_ffn_w3_w, L0_ffn_w3_s):
    global _NC_CACHE
    n_cores = 8
    asc = np.ascontiguousarray

    x = np.asarray(L0_attn_x, dtype=np.float32)
    xT_full = asc(x.reshape(16384, 1024).T)

    def tile_w(w, ncols):
        wt = np.asarray(w, np.float32).T
        n_i = wt.shape[0] // P
        n_og = wt.shape[1] // ncols
        return asc(wt.reshape(n_i, P, n_og, ncols)
                     .transpose(2, 1, 0, 3).reshape(n_og, P, n_i * ncols))

    wT = tile_w(L0_attn_W, 512)
    w1T = tile_w(L0_ffn_w1_w, 512)
    w2T = tile_w(L0_ffn_w2_w, 512)
    h1T = tile_w(L0_ffn_h1_w, 128)
    h2T = tile_w(L0_ffn_h2_w, 128)
    w3T = tile_w(L0_ffn_w3_w, 128)

    def col(s, n):
        return asc(np.asarray(s, np.float32).reshape(n, P).T)

    s1c = col(L0_ffn_w1_s, 32)
    s2c = col(L0_ffn_w2_s, 32)
    sh1c = col(L0_ffn_h1_s, 32)
    sh2c = col(L0_ffn_h2_s, 32)
    s3c = col(L0_ffn_w3_s, 8)
    biasc = col(L0_attn_bias, 8)

    ones = np.ones((P, P), dtype=BF16NP)
    triu = np.triu(np.ones((P, P), np.float32)).astype(BF16NP)

    if _NC_CACHE is None:
        _NC_CACHE = build_nc()
    nc = _NC_CACHE

    shared = dict(wT=wT, w1T=w1T, w2T=w2T, h1T=h1T, h2T=h2T, w3T=w3T,
                  s1c=s1c, s2c=s2c, sh1c=sh1c, sh2c=sh2c, s3c=s3c,
                  biasc=biasc, ones=ones, triu=triu)
    in_maps = []
    for k in range(n_cores):
        m = dict(shared)
        m["xT"] = asc(xT_full[:, k * TPC:(k + 1) * TPC])
        in_maps.append(m)

    res = run_bass_kernel_spmd(nc, in_maps, core_ids=list(range(n_cores)))
    outT = np.concatenate([np.asarray(res.results[k]["outT"]) for k in range(n_cores)],
                          axis=1)
    return asc(outT.T).reshape(8, 2048, 1024).astype(np.float32, copy=False)


# version 22
# speedup vs baseline: 1.0054x; 1.0054x over previous
import numpy as np
import ml_dtypes
import concourse.bass as bass
import concourse.mybir as mybir
from concourse.bass_utils import run_bass_kernel_spmd

BF16NP = ml_dtypes.bfloat16
F32 = mybir.dt.float32
BF16 = mybir.dt.bfloat16
AF = mybir.ActivationFunctionType
ALU = mybir.AluOpType

P = 128
C = 1024
NCH = 2
TPC = NCH * C
ND = 8
NH = 32
INV_D = 1.0 / 1024.0
EPS = 1e-5

ENGINES = ("sync", "gpsimd", "tensor", "vector", "scalar")


class Sem:
    __slots__ = ("h", "n")

    def __init__(self, h):
        self.h = h
        self.n = 0


class Plan:
    def __init__(self):
        self.ops = {e: [] for e in ENGINES}

    def add(self, eng, fn, waits=(), inc=None, inc_n=1):
        w = tuple((s, v) for w_ in waits if w_ is not None
                  for (s, v) in (w_,) if s is not None and v > 0)
        self.ops[eng].append((fn, w, inc, inc_n))
        if inc is not None:
            inc.n += inc_n
            return (inc, inc.n)
        return None


def build_nc():
    nc = bass.Bass("TRN2", target_bir_lowering=False)

    xT = nc.dram_tensor("xT", [1024, TPC], F32, kind="ExternalInput").ap()
    wT = nc.dram_tensor("wT", [1024, 1024], F32, kind="ExternalInput").ap()
    w1T = nc.dram_tensor("w1T", [1024, 4096], F32, kind="ExternalInput").ap()
    w2T = nc.dram_tensor("w2T", [1024, 4096], F32, kind="ExternalInput").ap()
    h1T = nc.dram_tensor("h1T", [4096, 4096], F32, kind="ExternalInput").ap()
    h2T = nc.dram_tensor("h2T", [4096, 4096], F32, kind="ExternalInput").ap()
    w3T = nc.dram_tensor("w3T", [4096, 1024], F32, kind="ExternalInput").ap()
    s1c = nc.dram_tensor("s1c", [P, 32], F32, kind="ExternalInput").ap()
    s2c = nc.dram_tensor("s2c", [P, 32], F32, kind="ExternalInput").ap()
    sh1c = nc.dram_tensor("sh1c", [P, 32], F32, kind="ExternalInput").ap()
    sh2c = nc.dram_tensor("sh2c", [P, 32], F32, kind="ExternalInput").ap()
    s3c = nc.dram_tensor("s3c", [P, 8], F32, kind="ExternalInput").ap()
    biasc = nc.dram_tensor("biasc", [P, 8], F32, kind="ExternalInput").ap()
    ones_d = nc.dram_tensor("ones", [P, P], BF16, kind="ExternalInput").ap()
    triu_d = nc.dram_tensor("triu", [P, P], BF16, kind="ExternalInput").ap()
    outT = nc.dram_tensor("outT", [1024, TPC], F32, kind="ExternalOutput").ap()

    plan = Plan()

    from contextlib import ExitStack
    with ExitStack() as stack:
        ec = stack.enter_context
        x_sb = ec(nc.sbuf_tensor("x_sb", [P, ND * C], F32))
        u_sb = ec(nc.sbuf_tensor("u_sb", [P, NH * C], BF16))
        v_sb = ec(nc.sbuf_tensor("v_sb", [P, NH * C], BF16))
        wb = ec(nc.sbuf_tensor("wb", [P, 4 * 4096], BF16))
        stat = ec(nc.sbuf_tensor("stat", [P, 3 * C], F32))
        ones_sb = ec(nc.sbuf_tensor("ones_sb", [P, P], BF16))
        triu_sb = ec(nc.sbuf_tensor("triu_sb", [P, P], BF16))
        s1_sb = ec(nc.sbuf_tensor("s1_sb", [P, 32], F32))
        s2_sb = ec(nc.sbuf_tensor("s2_sb", [P, 32], F32))
        sh1_sb = ec(nc.sbuf_tensor("sh1_sb", [P, 32], F32))
        sh2_sb = ec(nc.sbuf_tensor("sh2_sb", [P, 32], F32))
        s3_sb = ec(nc.sbuf_tensor("s3_sb", [P, 8], F32))
        bias_sb = ec(nc.sbuf_tensor("bias_sb", [P, 8], F32))
        cst = ec(nc.sbuf_tensor("cst", [P, 2], F32))
        ps0 = ec(nc.psum_tensor("ps0", [P, 1024], F32))
        ps1 = ec(nc.psum_tensor("ps1", [P, 1024], F32))
        ps2 = ec(nc.psum_tensor("ps2", [P, 1024], F32))
        ps3 = ec(nc.psum_tensor("ps3", [P, 1024], F32))
        h_sy = ec(nc.semaphore("s_sy"))
        h_cst = ec(nc.semaphore("s_cst"))
        h_xl = [ec(nc.semaphore("s_xl%d" % i)) for i in range(ND)]
        h_st = [ec(nc.semaphore("s_st%d" % i)) for i in range(ND)]
        h_ex = [ec(nc.semaphore("s_ex%d" % i)) for i in range(2)]
        h_w = [ec(nc.semaphore("s_w%d" % q)) for q in range(4)]
        h_pe = ec(nc.semaphore("s_pe"))
        h_sc = ec(nc.semaphore("s_sc"))
        h_ve = ec(nc.semaphore("s_ve"))
        block = ec(nc.Block())
        ps = [ps0, ps1, ps2, ps3]
        s_sy, s_pe, s_sc, s_ve = Sem(h_sy), Sem(h_pe), Sem(h_sc), Sem(h_ve)
        s_cst = Sem(h_cst)
        s_w = [Sem(h) for h in h_w]
        s_xl = [Sem(h) for h in h_xl]
        s_st = [Sem(h) for h in h_st]
        s_ex = [Sem(h) for h in h_ex]

        XN, XBF, X2, HA = 0, ND * C, 2 * ND * C, 3 * ND * C

        def x_t(i):
            return x_sb[:, i * C:(i + 1) * C]

        def x_th(i, h):
            return x_sb[:, i * C + h * 512: i * C + (h + 1) * 512]

        def vre(off, i):
            return v_sb[:, off + i * C: off + (i + 1) * C]

        def vre_h(off, i, h):
            return v_sb[:, off + i * C + h * 512: off + i * C + (h + 1) * 512]

        def u_t(o):
            return u_sb[:, o * C:(o + 1) * C]

        def u_th(i, h):
            return u_sb[:, i * C + h * 512: i * C + (h + 1) * 512]

        def v_t(o):
            return v_sb[:, o * C:(o + 1) * C]

        def v_th(i, h):
            return v_sb[:, i * C + h * 512: i * C + (h + 1) * 512]

        ps_free = [None, None, None, None]
        ps_seq = [0, 0, 0, 0]
        ps_ctr = [0]

        def psum_acquire(pool=(0, 1, 2, 3)):
            k = min(pool, key=lambda j: ps_seq[j])
            ps_ctr[0] += 1
            ps_seq[k] = ps_ctr[0]
            return k, ps_free[k]

        dma_fns = []
        dma_ready = []
        dma_mm_done = []
        dma_planned = [0]

        def enqueue_wdma(dram, n_i, ncols, og):
            def mk(q, dram=dram, n_i=n_i, ncols=ncols, og=og):
                def fn(e):
                    dst = wb[:, q * 4096: q * 4096 + n_i * ncols].rearrange(
                        "p (a c) -> p a c", c=ncols)
                    src = dram.rearrange("(a p) o -> p a o", p=P)[:, :, og * ncols:(og + 1) * ncols]
                    return e.dma_start(out=dst, in_=src)
                return fn
            idx = len(dma_fns)
            dma_fns.append(mk)
            dma_ready.append(None)
            dma_mm_done.append(None)
            return idx

        def pump_dmas():
            while dma_planned[0] < len(dma_fns):
                i = dma_planned[0]
                if i >= 4 and dma_mm_done[i - 4] is None:
                    break
                q = i % 4
                waits = [] if i < 4 else [dma_mm_done[i - 4]]
                ev = plan.add("gpsimd", dma_fns[i](q), waits=waits, inc=s_w[q], inc_n=16)
                dma_ready[i] = ev
                dma_planned[0] += 1

        def slot_of(idx):
            return idx % 4

        def mm_group(k, lhs_aps, rhs_fn, first_waits, per_i_waits=None):
            n = len(lhs_aps)
            ev = None
            for i in range(n):
                for h in range(2):
                    waits = []
                    if i == 0 and h == 0:
                        waits += list(first_waits)
                    if per_i_waits is not None and h == 0:
                        w = per_i_waits(i)
                        if w:
                            waits += w
                    lhs = lhs_aps[i]
                    rhs = rhs_fn(i, h)
                    out = ps[k][:, h * 512:(h + 1) * 512]
                    is_last = (i == n - 1 and h == 1)

                    def fn(e, out=out, lhs=lhs, rhs=rhs, st=(i == 0), sp=(i == n - 1)):
                        return e.matmul(out, lhs, rhs, start=st, stop=sp)
                    if is_last:
                        ev = plan.add("tensor", fn, waits=waits, inc=s_pe)
                    else:
                        plan.add("tensor", fn, waits=waits)
            return ev

        def dma_to(dst, src):
            def fn(e, dst=dst, src=src):
                return e.dma_start(out=dst, in_=src)
            return fn

        consts_evt = None
        for dst, src in (
            (ones_sb[:, :], ones_d), (triu_sb[:, :], triu_d),
            (s1_sb[:, :], s1c), (s2_sb[:, :], s2c),
            (sh1_sb[:, :], sh1c), (sh2_sb[:, :], sh2c),
            (s3_sb[:, :], s3c), (bias_sb[:, :], biasc),
        ):
            consts_evt = plan.add("scalar", dma_to(dst, src), inc=s_cst, inc_n=16)

        plan.add("vector", lambda e: e.memset(cst[:, 0:1], EPS))
        cst_evt = plan.add("vector", lambda e: e.memset(cst[:, 1:2], -0.5), inc=s_ve)

        stat_m = stat[:, 0:C]
        stat_msq = stat[:, C:2 * C]
        stat_r = stat[:, 2 * C:3 * C]

        def emit_acc_add(acc_ap, src_tile, i, waits):
            if i == 0:
                def f(e, a=acc_ap, s=src_tile):
                    return e.tensor_copy(a, s)
            else:
                def f(e, a=acc_ap, s=src_tile):
                    return e.tensor_add(a, a, s)
            return plan.add("vector", f, waits=waits, inc=s_ve)

        def ln_phase(x_ready, war_xbf_x2, war_xn, war_tmp, war_ha,
                     pool=(0, 1, 2, 3), make_xn=True, precast=None):
            aS_evt = aSS_evt = None
            if precast is not None:
                if len(precast) == 4:
                    xbf_evt, x2_evt, aS_evt, aSS_evt = precast
                else:
                    xbf_evt, x2_evt = precast
            else:
                xbf_evt = [None] * ND
                x2_evt = [None] * ND
                for i in range(ND):
                    def fc(e, i=i):
                        return e.tensor_copy(vre(XBF, i), x_t(i))
                    xbf_evt[i] = plan.add("vector", fc,
                                          waits=x_ready(i) + [war_xbf_x2], inc=s_ve)
                    if not make_xn and i < ND - 1:
                        aS_evt = emit_acc_add(vre(XN, 1), vre(XBF, i), i,
                                              [xbf_evt[i], war_xn, aS_evt])
                for i in range(ND):
                    def fs(e, i=i):
                        return e.activation(vre(X2, i), x_t(i), AF.Square)
                    x2_evt[i] = plan.add("scalar", fs,
                                         waits=x_ready(i) + [war_xbf_x2], inc=s_sc)
                    if not make_xn and i < ND - 1:
                        aSS_evt = emit_acc_add(vre(XN, 2), vre(X2, i), i,
                                               [x2_evt[i], war_xn, aSS_evt])

            if not make_xn and aS_evt is not None:
                def stats_group(k, acc_ap, t7_off, a_evt, t7_evt, free_evt):
                    last = None
                    for hh in range(2):
                        w1_ = [free_evt, consts_evt, a_evt] if hh == 0 else []
                        def fa(e, k=k, hh=hh, a=acc_ap):
                            return e.matmul(ps[k][:, hh * 512:(hh + 1) * 512],
                                            ones_sb[:, 0:P],
                                            a[:, hh * 512:(hh + 1) * 512],
                                            start=True, stop=False)
                        plan.add("tensor", fa, waits=w1_)
                        w2_ = [t7_evt] if hh == 0 else []
                        def fb(e, k=k, hh=hh, t7_off=t7_off):
                            return e.matmul(ps[k][:, hh * 512:(hh + 1) * 512],
                                            ones_sb[:, 0:P],
                                            vre_h(t7_off, ND - 1, hh),
                                            start=False, stop=True)
                        if hh == 1:
                            last = plan.add("tensor", fb, waits=w2_, inc=s_pe)
                        else:
                            plan.add("tensor", fb, waits=w2_)
                    return last

                kS, freeS = psum_acquire(pool)
                evS = stats_group(kS, vre(XN, 1), XBF, aS_evt, xbf_evt[ND - 1], freeS)
                kSS, freeSS = psum_acquire(pool)
                evSS = stats_group(kSS, vre(XN, 2), X2, aSS_evt, x2_evt[ND - 1], freeSS)
            else:
                kS, freeS = psum_acquire(pool)
                evS = mm_group(kS, [ones_sb[:, 0:P]] * ND,
                               lambda i, h: vre_h(XBF, i, h),
                               first_waits=[freeS, consts_evt],
                               per_i_waits=lambda i: [xbf_evt[i]])
                kSS, freeSS = psum_acquire(pool)
                evSS = mm_group(kSS, [ones_sb[:, 0:P]] * ND,
                                lambda i, h: vre_h(X2, i, h),
                                first_waits=[freeSS, consts_evt],
                                per_i_waits=lambda i: [x2_evt[i]])

            m_ap, msq_ap, r_ap = stat_m, stat_msq, stat_r
            mbf_ap = vre(HA, 0)
            rbf_ap = vre(HA, 1)

            ev_m = plan.add("vector",
                            lambda e, kS=kS: e.tensor_scalar_mul(m_ap, ps[kS][:, 0:C], INV_D),
                            waits=[evS, (s_sc, s_sc.n)], inc=s_ve)
            ps_free[kS] = ev_m
            ev_msq = plan.add("scalar",
                              lambda e: e.activation(msq_ap, m_ap, AF.Square),
                              waits=[ev_m], inc=s_sc)
            mbf_evt = plan.add("vector", lambda e: e.tensor_copy(mbf_ap, m_ap),
                     waits=[war_ha, ev_m], inc=s_ve)
            sub_evt = [None] * ND
            n_pre = ND if not make_xn else 3

            def emit_subs(rng):
                for i in rng:
                    def fsub(e, i=i):
                        return e.tensor_sub(u_t(i), vre(XBF, i), mbf_ap)
                    sub_evt[i] = plan.add("vector", fsub,
                                          waits=[war_tmp, mbf_evt], inc=s_ve)

            emit_subs(range(n_pre))
            ev_var = plan.add("vector",
                              lambda e, kSS=kSS: e.scalar_tensor_tensor(
                                  r_ap, ps[kSS][:, 0:C], INV_D, msq_ap,
                                  ALU.mult, ALU.subtract),
                              waits=[evSS, ev_msq], inc=s_ve)
            ps_free[kSS] = ev_var
            ln_evt = plan.add("scalar",
                              lambda e: e.activation(r_ap, r_ap, AF.Ln, bias=cst[:, 0:1]),
                              waits=[ev_var, cst_evt], inc=s_sc)
            rstd_evt = plan.add("scalar",
                                lambda e: e.activation(r_ap, r_ap, AF.Exp, scale=cst[:, 1:2]),
                                waits=[ln_evt], inc=s_sc)
            emit_subs(range(n_pre, ND))
            if not make_xn:
                return sub_evt, rstd_evt, ev_msq
            def warm(gate, n=4, kS=None):
                for _ in range(n):
                    def fd(e, kS=kS):
                        return e.matmul(ps[kS][:, 0:128], ones_sb[:, 0:P],
                                        ones_sb[:, 0:P], start=True, stop=True)
                    plan.add("tensor", fd, waits=[gate])
            warm(ev_var, kS=kS)
            warm(rstd_evt, kS=kS)
            rbf_evt = plan.add("vector", lambda e: e.tensor_copy(rbf_ap, r_ap),
                                waits=[rstd_evt], inc=s_ve)
            warm(rbf_evt, kS=kS)
            xn_evt = [None] * ND
            for i in range(ND):
                def fmul(e, i=i):
                    return e.tensor_mul(vre(XN, i), u_t(i), rbf_ap)
                xn_evt[i] = plan.add("vector", fmul,
                                     waits=[war_xn, rbf_evt, sub_evt[i]], inc=s_ve)
            return xn_evt, rstd_evt, ev_msq

        store_done = None
        war_u = None
        war_ha_ve = None
        attn_end_pe = None
        w1w2_end_pe = None
        h1_end_pe = None
        h2_end_pe = None
        u_all_evt = None

        next_pre = None
        for cidx in range(NCH):
            c0 = cidx * C

            if next_pre is None:
                xl_evt = [None] * ND
                for i in range(ND):
                    def fxl(e, i=i, c0=c0):
                        return e.dma_start(out=x_t(i),
                                           in_=xT[i * P:(i + 1) * P, c0:c0 + C])
                    xl_evt[i] = plan.add("sync", fxl, waits=[],
                                         inc=s_xl[i], inc_n=16)
                pre_cast = None
            else:
                xl_evt = next_pre["xl"]
                pre_cast = (next_pre["xbf"], next_pre["x2"],
                            next_pre["accS"], next_pre["accSS"])
                next_pre = None

            attn_dma = [enqueue_wdma(wT, ND, 512, og) for og in range(2)]
            w12_dma = []
            for og in range(8):
                w12_dma.append((enqueue_wdma(w1T, ND, 512, og),
                                enqueue_wdma(w2T, ND, 512, og)))
            h1_dma = [enqueue_wdma(h1T, NH, 128, o) for o in range(NH)]
            h2_dma = [enqueue_wdma(h2T, NH, 128, o) for o in range(NH)]
            w3_dma = [enqueue_wdma(w3T, NH, 128, o) for o in range(ND)]
            pump_dmas()

            sub_evt, rstd1_evt, msq1_evt = ln_phase(
                lambda i: [xl_evt[i]],
                war_xbf_x2=h2_end_pe, war_xn=w1w2_end_pe,
                war_tmp=war_u, war_ha=war_u, pool=(2, 3), make_xn=False,
                precast=pre_cast)

            ha_evt = [None] * ND
            xadd_evt = [None] * ND
            tt_evt = [None] * ND
            xbf2_evt = [None] * ND
            x22_evt = [None] * ND
            sbf_state = [None]

            def emit_attn(o):
                og = o // 4
                j = o % 4
                idx = attn_dma[og]
                q = slot_of(idx)
                blocks = [wb[:, q * 4096 + i * 512 + j * P: q * 4096 + i * 512 + j * P + P]
                          for i in range(ND)]
                kH, freeH = psum_acquire()
                evH = mm_group(kH, blocks, lambda i, h: u_th(i, h),
                               first_waits=[freeH, dma_ready[idx]],
                               per_i_waits=lambda i: [sub_evt[i]])
                if j == 3:
                    dma_mm_done[idx] = (s_pe, s_pe.n)
                    pump_dmas()

                scr = stat_m if o % 2 == 0 else stat_msq
                scr_war = [msq1_evt] if o < 2 else [ha_evt[o - 2]]
                def ftt(e, kH=kH, scr=scr):
                    return e.tensor_tensor(scr, ps[kH][:, 0:C], stat_r, ALU.mult)
                tt_evt[o] = plan.add("vector", ftt,
                                     waits=[evH, rstd1_evt] + scr_war, inc=s_ve)
                ps_free[kH] = tt_evt[o]

                ha_war = [war_ha_ve] if (war_ha_ve and o < 2) else []
                def fha(e, o=o, scr=scr):
                    return e.activation(vre(HA, o), scr, AF.Identity,
                                        bias=bias_sb[:, o:o + 1])
                ha_evt[o] = plan.add("scalar", fha,
                                     waits=[tt_evt[o], consts_evt] + ha_war, inc=s_sc)

            def emit_y(o):
                kY, freeY = psum_acquire()
                evY = None
                for hh in range(2):
                    waits = [freeY, consts_evt, ha_evt[o]] if hh == 0 else []
                    def f1(e, kY=kY, o=o, hh=hh):
                        return e.matmul(ps[kY][:, hh * 512:(hh + 1) * 512],
                                        triu_sb[:, 0:P], vre_h(HA, o, hh),
                                        start=True, stop=(o == 0))
                    if o == 0 and hh == 1:
                        evY = plan.add("tensor", f1, waits=waits, inc=s_pe)
                    else:
                        plan.add("tensor", f1, waits=waits)
                if o > 0:
                    for hh in range(2):
                        waits = [sbf_state[0]] if hh == 0 else []
                        def f2(e, kY=kY, hh=hh):
                            return e.matmul(ps[kY][:, hh * 512:(hh + 1) * 512],
                                            ones_sb[:, 0:P],
                                            v_sb[:, XN + hh * 512: XN + (hh + 1) * 512],
                                            start=False, stop=True)
                        if hh == 1:
                            evY = plan.add("tensor", f2, waits=waits, inc=s_pe)
                        else:
                            plan.add("tensor", f2, waits=waits)
                def fadd(e, o=o, kY=kY):
                    return e.tensor_add(x_t(o), x_t(o), ps[kY][:, 0:C])
                xadd_evt[o] = plan.add("vector", fadd,
                                       waits=[evY, xl_evt[o]], inc=s_ve)
                ps_free[kY] = xadd_evt[o]

                if o == 0:
                    sbf_state[0] = plan.add(
                        "vector",
                        lambda e: e.tensor_copy(vre(XN, 0), vre(HA, 0)),
                        waits=[ha_evt[0]] + ([w1w2_end_pe] if w1w2_end_pe else []),
                        inc=s_ve)
                elif o < ND - 1:
                    def fsu(e, o=o):
                        return e.tensor_add(vre(XN, 0), vre(XN, 0), vre(HA, o))
                    sbf_state[0] = plan.add("vector", fsu,
                                            waits=[ha_evt[o], evY, sbf_state[0]],
                                            inc=s_ve)

                def fc2(e, o=o):
                    return e.tensor_copy(vre(XBF, o), x_t(o))
                xbf2_evt[o] = plan.add("vector", fc2, waits=[xadd_evt[o]],
                                       inc=s_ve)
                def fs2(e, o=o):
                    return e.activation(vre(X2, o), x_t(o), AF.Square)
                x22_evt[o] = plan.add("scalar", fs2, waits=[xadd_evt[o]],
                                      inc=s_sc)

            for o in range(ND):
                emit_attn(o)
                if o >= 1:
                    emit_y(o - 1)
            emit_y(ND - 1)
            attn_end_pe = (s_pe, s_pe.n)

            xn2_evt, rstd2_evt, msq2_evt = ln_phase(lambda i: [xadd_evt[i]],
                                     war_xbf_x2=attn_end_pe,
                                     war_xn=attn_end_pe,
                                     war_tmp=attn_end_pe, war_ha=attn_end_pe,
                                     precast=(xbf2_evt, x22_evt))
            ln2_stats_end = (s_pe, s_pe.n)
            ln2_ve_end = (s_ve, s_ve.n)
            ln2_sc_end = (s_sc, s_sc.n)

            u_evt = [None] * NH
            for og in range(8):
                idx1, idx2 = w12_dma[og]
                q1, q2 = slot_of(idx1), slot_of(idx2)
                for j in range(4):
                    o = og * 4 + j
                    b1 = [wb[:, q1 * 4096 + i * 512 + j * P: q1 * 4096 + i * 512 + j * P + P]
                          for i in range(ND)]
                    b2 = [wb[:, q2 * 4096 + i * 512 + j * P: q2 * 4096 + i * 512 + j * P + P]
                          for i in range(ND)]
                    kA, freeA = psum_acquire()
                    evA = mm_group(kA, b1, lambda i, h: vre_h(XN, i, h),
                                   first_waits=[freeA, dma_ready[idx1]],
                                   per_i_waits=lambda i: [xn2_evt[i]])
                    kB, freeB = psum_acquire()
                    evB = mm_group(kB, b2, lambda i, h: vre_h(XN, i, h),
                                   first_waits=[freeB, dma_ready[idx2]])

                    at = o % 2
                    if o >= 2:
                        a_war = [u_evt[o - 2]]
                    else:
                        a_war = [attn_end_pe, war_ha_ve]

                    def fsilu(e, at=at, kA=kA, o=o):
                        return e.activation(vre(HA, at), ps[kA][:, 0:C], AF.Silu,
                                            scale=s1_sb[:, o:o + 1])
                    a_evt = plan.add("scalar", fsilu, waits=[evA] + a_war, inc=s_sc)
                    ps_free[kA] = a_evt

                    u_war = [war_u] if (war_u and o < 1) else []
                    def fu(e, o=o, kB=kB, at=at):
                        return e.scalar_tensor_tensor(u_t(o), ps[kB][:, 0:C],
                                                      s2_sb[:, o:o + 1], vre(HA, at),
                                                      ALU.mult, ALU.mult)
                    u_evt[o] = plan.add("vector", fu, waits=[evB, a_evt] + u_war, inc=s_ve)
                    ps_free[kB] = u_evt[o]
                dma_mm_done[idx1] = dma_mm_done[idx2] = (s_pe, s_pe.n)
                pump_dmas()
            w1w2_end_pe = (s_pe, s_pe.n)
            u_all_evt = u_evt[NH - 1]
            war_ha_ve = u_evt[NH - 1]

            v_evt = [None] * NH
            for o in range(NH):
                idx = h1_dma[o]
                q = slot_of(idx)
                blocks = [wb[:, q * 4096 + i * P: q * 4096 + (i + 1) * P]
                          for i in range(NH)]
                kP, freeP = psum_acquire()
                ev = mm_group(kP, blocks, lambda i, h: u_th(i, h),
                              first_waits=[freeP, dma_ready[idx]],
                              per_i_waits=lambda i: [u_evt[i]])
                dma_mm_done[idx] = (s_pe, s_pe.n)
                pump_dmas()
                v_war = [w1w2_end_pe, war_ha_ve] if o < 1 else []
                def fg(e, o=o, kP=kP):
                    return e.activation(v_t(o), ps[kP][:, 0:C], AF.Gelu,
                                        scale=sh1_sb[:, o:o + 1])
                v_evt[o] = plan.add("scalar", fg, waits=[ev] + v_war, inc=s_sc)
                ps_free[kP] = v_evt[o]
            h1_end_pe = (s_pe, s_pe.n)

            w_evt = [None] * NH
            for o in range(NH):
                idx = h2_dma[o]
                q = slot_of(idx)
                blocks = [wb[:, q * 4096 + i * P: q * 4096 + (i + 1) * P]
                          for i in range(NH)]
                kP, freeP = psum_acquire()
                ev = mm_group(kP, blocks, lambda i, h: v_th(i, h),
                              first_waits=[freeP, dma_ready[idx]],
                              per_i_waits=lambda i: [v_evt[i]])
                dma_mm_done[idx] = (s_pe, s_pe.n)
                pump_dmas()
                w_war = [h1_end_pe] if o < 1 else []
                def fg2(e, o=o, kP=kP):
                    return e.activation(u_t(o), ps[kP][:, 0:C], AF.Gelu,
                                        scale=sh2_sb[:, o:o + 1])
                w_evt[o] = plan.add("scalar", fg2, waits=[ev] + w_war, inc=s_sc)
                ps_free[kP] = w_evt[o]
            h2_end_pe = (s_pe, s_pe.n)

            xout_evt = [None] * ND
            st_evt = [None] * ND
            ex_evt = [None, None]
            if cidx + 1 < NCH:
                next_pre = {"xl": [None] * ND, "xbf": [None] * ND,
                            "x2": [None] * ND, "accS": None, "accSS": None}
                c1e = (cidx + 1) * C
                for t_, dst_ in ((6, stat_m), (7, stat_msq)):
                    def fex(e, t_=t_, dst_=dst_, c1e=c1e):
                        return e.dma_start(out=dst_,
                                           in_=xT[t_ * P:(t_ + 1) * P, c1e:c1e + C])
                    ex_evt[t_ - 6] = plan.add("scalar", fex,
                                              waits=[ln2_ve_end, ln2_sc_end],
                                              inc=s_ex[t_ - 6], inc_n=16)
            for o in range(ND):
                idx = w3_dma[o]
                q = slot_of(idx)
                blocks = [wb[:, q * 4096 + i * P: q * 4096 + (i + 1) * P]
                          for i in range(NH)]
                kP, freeP = psum_acquire(pool=(0, 1))
                ev = mm_group(kP, blocks, lambda i, h: u_th(i, h),
                              first_waits=[freeP, dma_ready[idx]],
                              per_i_waits=lambda i: [w_evt[i]])
                dma_mm_done[idx] = (s_pe, s_pe.n)
                pump_dmas()
                def fw3(e, o=o, kP=kP):
                    return e.scalar_tensor_tensor(x_t(o), ps[kP][:, 0:C],
                                                  s3_sb[:, o:o + 1], x_t(o),
                                                  ALU.mult, ALU.add)
                xout_evt[o] = plan.add("vector", fw3, waits=[ev], inc=s_ve)
                ps_free[kP] = xout_evt[o]

                def fst(e, o=o, c0=c0):
                    return e.dma_start(out=outT[o * P:(o + 1) * P, c0:c0 + C],
                                       in_=x_t(o))
                st_evt[o] = plan.add("sync", fst, waits=[xout_evt[o]],
                                     inc=s_st[o], inc_n=16)

                if cidx + 1 < NCH:
                    c1 = (cidx + 1) * C
                    def fxl2(e, o=o, c1=c1):
                        return e.dma_start(out=x_t(o),
                                           in_=xT[o * P:(o + 1) * P, c1:c1 + C])
                    next_pre["xl"][o] = plan.add("sync", fxl2,
                                                 waits=[st_evt[o]],
                                                 inc=s_xl[o], inc_n=16)
                    xsrc = x_t(o) if o < 6 else (stat_m if o == 6 else stat_msq)
                    src_rdy = ([next_pre["xl"][o]] if o < 6 else [ex_evt[o - 6]])
                    def fc3(e, o=o, xsrc=xsrc):
                        return e.tensor_copy(vre(XBF, o), xsrc)
                    next_pre["xbf"][o] = plan.add(
                        "vector", fc3,
                        waits=src_rdy + [ln2_stats_end], inc=s_ve)
                    def fs3(e, o=o, xsrc=xsrc):
                        return e.activation(vre(X2, o), xsrc, AF.Square)
                    next_pre["x2"][o] = plan.add(
                        "scalar", fs3,
                        waits=src_rdy + [ln2_stats_end], inc=s_sc)
                    if o < ND - 1:
                        next_pre["accS"] = emit_acc_add(
                            vre(XN, 1), vre(XBF, o), o,
                            [next_pre["xbf"][o], next_pre["accS"],
                             w1w2_end_pe])
                        next_pre["accSS"] = emit_acc_add(
                            vre(XN, 2), vre(X2, o), o,
                            [next_pre["x2"][o], next_pre["accSS"],
                             w1w2_end_pe])
            war_u = (s_pe, s_pe.n)
            store_done = list(st_evt)

        plan.add("sync", None, waits=list(store_done))

        def emitter(eng_name):
            def run(e):
                waited = {}
                for fn, waits, inc, inc_n in plan.ops[eng_name]:
                    for (s, v) in waits:
                        key = id(s)
                        if waited.get(key, 0) < v:
                            e.wait_ge(s.h, v)
                            waited[key] = v
                    if fn is None:
                        continue
                    inst = fn(e)
                    if inc is not None:
                        inst.then_inc(inc.h, inc_n)
            return run

        block.sync(emitter("sync"))
        block.gpsimd(emitter("gpsimd"))
        block.tensor(emitter("tensor"))
        block.vector(emitter("vector"))
        block.scalar(emitter("scalar"))

    return nc


_NC_CACHE = None


def kernel(L0_attn_x, L0_attn_W, L0_attn_bias,
           L0_ffn_w1_w, L0_ffn_w1_s, L0_ffn_w2_w, L0_ffn_w2_s,
           L0_ffn_h1_w, L0_ffn_h1_s, L0_ffn_h2_w, L0_ffn_h2_s,
           L0_ffn_w3_w, L0_ffn_w3_s):
    global _NC_CACHE
    n_cores = 8
    asc = np.ascontiguousarray

    x = np.asarray(L0_attn_x, dtype=np.float32)
    xT_full = asc(x.reshape(16384, 1024).T)

    wT = asc(np.asarray(L0_attn_W, np.float32).T)
    w1T = asc(np.asarray(L0_ffn_w1_w, np.float32).T)
    w2T = asc(np.asarray(L0_ffn_w2_w, np.float32).T)
    h1T = asc(np.asarray(L0_ffn_h1_w, np.float32).T)
    h2T = asc(np.asarray(L0_ffn_h2_w, np.float32).T)
    w3T = asc(np.asarray(L0_ffn_w3_w, np.float32).T)

    def col(s, n):
        return asc(np.asarray(s, np.float32).reshape(n, P).T)

    s1c = col(L0_ffn_w1_s, 32)
    s2c = col(L0_ffn_w2_s, 32)
    sh1c = col(L0_ffn_h1_s, 32)
    sh2c = col(L0_ffn_h2_s, 32)
    s3c = col(L0_ffn_w3_s, 8)
    biasc = col(L0_attn_bias, 8)

    ones = np.ones((P, P), dtype=BF16NP)
    triu = np.triu(np.ones((P, P), np.float32)).astype(BF16NP)

    if _NC_CACHE is None:
        _NC_CACHE = build_nc()
    nc = _NC_CACHE

    shared = dict(wT=wT, w1T=w1T, w2T=w2T, h1T=h1T, h2T=h2T, w3T=w3T,
                  s1c=s1c, s2c=s2c, sh1c=sh1c, sh2c=sh2c, s3c=s3c,
                  biasc=biasc, ones=ones, triu=triu)
    in_maps = []
    for k in range(n_cores):
        m = dict(shared)
        m["xT"] = asc(xT_full[:, k * TPC:(k + 1) * TPC])
        in_maps.append(m)

    res = run_bass_kernel_spmd(nc, in_maps, core_ids=list(range(n_cores)))
    outT = np.concatenate([np.asarray(res.results[k]["outT"]) for k in range(n_cores)],
                          axis=1)
    return asc(outT.T).reshape(8, 2048, 1024).astype(np.float32, copy=False)


# version 23 (final)
# speedup vs baseline: 1.0080x; 1.0026x over previous
import numpy as np
import ml_dtypes
import concourse.bass as bass
import concourse.mybir as mybir
from concourse.bass_utils import run_bass_kernel_spmd

BF16NP = ml_dtypes.bfloat16
F32 = mybir.dt.float32
BF16 = mybir.dt.bfloat16
AF = mybir.ActivationFunctionType
ALU = mybir.AluOpType

P = 128
C = 1024
NCH = 2
TPC = NCH * C
ND = 8
NH = 32
INV_D = 1.0 / 1024.0
EPS = 1e-5

ENGINES = ("sync", "gpsimd", "tensor", "vector", "scalar")


class Sem:
    __slots__ = ("h", "n")

    def __init__(self, h):
        self.h = h
        self.n = 0


class Plan:
    def __init__(self):
        self.ops = {e: [] for e in ENGINES}

    def add(self, eng, fn, waits=(), inc=None, inc_n=1):
        w = tuple((s, v) for w_ in waits if w_ is not None
                  for (s, v) in (w_,) if s is not None and v > 0)
        self.ops[eng].append((fn, w, inc, inc_n))
        if inc is not None:
            inc.n += inc_n
            return (inc, inc.n)
        return None


def build_nc():
    nc = bass.Bass("TRN2", target_bir_lowering=False)

    xT = nc.dram_tensor("xT", [1024, TPC], F32, kind="ExternalInput").ap()
    wT = nc.dram_tensor("wT", [1024, 1024], F32, kind="ExternalInput").ap()
    w1T = nc.dram_tensor("w1T", [1024, 4096], F32, kind="ExternalInput").ap()
    w2T = nc.dram_tensor("w2T", [1024, 4096], F32, kind="ExternalInput").ap()
    h1T = nc.dram_tensor("h1T", [4096, 4096], F32, kind="ExternalInput").ap()
    h2T = nc.dram_tensor("h2T", [4096, 4096], F32, kind="ExternalInput").ap()
    w3T = nc.dram_tensor("w3T", [4096, 1024], F32, kind="ExternalInput").ap()
    s1c = nc.dram_tensor("s1c", [P, 32], F32, kind="ExternalInput").ap()
    s2c = nc.dram_tensor("s2c", [P, 32], F32, kind="ExternalInput").ap()
    sh1c = nc.dram_tensor("sh1c", [P, 32], F32, kind="ExternalInput").ap()
    sh2c = nc.dram_tensor("sh2c", [P, 32], F32, kind="ExternalInput").ap()
    s3c = nc.dram_tensor("s3c", [P, 8], F32, kind="ExternalInput").ap()
    biasc = nc.dram_tensor("biasc", [P, 8], F32, kind="ExternalInput").ap()
    ones_d = nc.dram_tensor("ones", [P, P], BF16, kind="ExternalInput").ap()
    triu_d = nc.dram_tensor("triu", [P, P], BF16, kind="ExternalInput").ap()
    outT = nc.dram_tensor("outT", [1024, TPC], F32, kind="ExternalOutput").ap()

    plan = Plan()

    from contextlib import ExitStack
    with ExitStack() as stack:
        ec = stack.enter_context
        x_sb = ec(nc.sbuf_tensor("x_sb", [P, ND * C], F32))
        u_sb = ec(nc.sbuf_tensor("u_sb", [P, NH * C], BF16))
        v_sb = ec(nc.sbuf_tensor("v_sb", [P, NH * C], BF16))
        wb = ec(nc.sbuf_tensor("wb", [P, 4 * 4096], BF16))
        stat = ec(nc.sbuf_tensor("stat", [P, 3 * C], F32))
        ones_sb = ec(nc.sbuf_tensor("ones_sb", [P, P], BF16))
        triu_sb = ec(nc.sbuf_tensor("triu_sb", [P, P], BF16))
        s1_sb = ec(nc.sbuf_tensor("s1_sb", [P, 32], F32))
        s2_sb = ec(nc.sbuf_tensor("s2_sb", [P, 32], F32))
        sh1_sb = ec(nc.sbuf_tensor("sh1_sb", [P, 32], F32))
        sh2_sb = ec(nc.sbuf_tensor("sh2_sb", [P, 32], F32))
        s3_sb = ec(nc.sbuf_tensor("s3_sb", [P, 8], F32))
        bias_sb = ec(nc.sbuf_tensor("bias_sb", [P, 8], F32))
        cst = ec(nc.sbuf_tensor("cst", [P, 2], F32))
        junk = ec(nc.sbuf_tensor("junk", [P, 1], F32))
        ps0 = ec(nc.psum_tensor("ps0", [P, 1024], F32))
        ps1 = ec(nc.psum_tensor("ps1", [P, 1024], F32))
        ps2 = ec(nc.psum_tensor("ps2", [P, 1024], F32))
        ps3 = ec(nc.psum_tensor("ps3", [P, 1024], F32))
        h_sy = ec(nc.semaphore("s_sy"))
        h_cst = ec(nc.semaphore("s_cst"))
        h_xl = [ec(nc.semaphore("s_xl%d" % i)) for i in range(ND)]
        h_st = [ec(nc.semaphore("s_st%d" % i)) for i in range(ND)]
        h_ex = [ec(nc.semaphore("s_ex%d" % i)) for i in range(2)]
        h_w = [ec(nc.semaphore("s_w%d" % q)) for q in range(4)]
        h_pe = ec(nc.semaphore("s_pe"))
        h_sc = ec(nc.semaphore("s_sc"))
        h_ve = ec(nc.semaphore("s_ve"))
        block = ec(nc.Block())
        ps = [ps0, ps1, ps2, ps3]
        s_sy, s_pe, s_sc, s_ve = Sem(h_sy), Sem(h_pe), Sem(h_sc), Sem(h_ve)
        s_cst = Sem(h_cst)
        s_w = [Sem(h) for h in h_w]
        s_xl = [Sem(h) for h in h_xl]
        s_st = [Sem(h) for h in h_st]
        s_ex = [Sem(h) for h in h_ex]

        XN, XBF, X2, HA = 0, ND * C, 2 * ND * C, 3 * ND * C

        def x_t(i):
            return x_sb[:, i * C:(i + 1) * C]

        def x_th(i, h):
            return x_sb[:, i * C + h * 512: i * C + (h + 1) * 512]

        def vre(off, i):
            return v_sb[:, off + i * C: off + (i + 1) * C]

        def vre_h(off, i, h):
            return v_sb[:, off + i * C + h * 512: off + i * C + (h + 1) * 512]

        def u_t(o):
            return u_sb[:, o * C:(o + 1) * C]

        def u_th(i, h):
            return u_sb[:, i * C + h * 512: i * C + (h + 1) * 512]

        def v_t(o):
            return v_sb[:, o * C:(o + 1) * C]

        def v_th(i, h):
            return v_sb[:, i * C + h * 512: i * C + (h + 1) * 512]

        ps_free = [None, None, None, None]
        ps_seq = [0, 0, 0, 0]
        ps_ctr = [0]

        def psum_acquire(pool=(0, 1, 2, 3)):
            k = min(pool, key=lambda j: ps_seq[j])
            ps_ctr[0] += 1
            ps_seq[k] = ps_ctr[0]
            return k, ps_free[k]

        dma_fns = []
        dma_ready = []
        dma_mm_done = []
        dma_planned = [0]

        def enqueue_wdma(dram, n_i, ncols, og):
            def mk(q, dram=dram, n_i=n_i, ncols=ncols, og=og):
                def fn(e):
                    dst = wb[:, q * 4096: q * 4096 + n_i * ncols].rearrange(
                        "p (a c) -> p a c", c=ncols)
                    src = dram.rearrange("(a p) o -> p a o", p=P)[:, :, og * ncols:(og + 1) * ncols]
                    return e.dma_start(out=dst, in_=src)
                return fn
            idx = len(dma_fns)
            dma_fns.append(mk)
            dma_ready.append(None)
            dma_mm_done.append(None)
            return idx

        def pump_dmas():
            while dma_planned[0] < len(dma_fns):
                i = dma_planned[0]
                if i >= 4 and dma_mm_done[i - 4] is None:
                    break
                q = i % 4
                waits = [] if i < 4 else [dma_mm_done[i - 4]]
                ev = plan.add("gpsimd", dma_fns[i](q), waits=waits, inc=s_w[q], inc_n=16)
                dma_ready[i] = ev
                dma_planned[0] += 1

        def slot_of(idx):
            return idx % 4

        def mm_group(k, lhs_aps, rhs_fn, first_waits, per_i_waits=None):
            n = len(lhs_aps)
            ev = None
            for i in range(n):
                for h in range(2):
                    waits = []
                    if i == 0 and h == 0:
                        waits += list(first_waits)
                    if per_i_waits is not None and h == 0:
                        w = per_i_waits(i)
                        if w:
                            waits += w
                    lhs = lhs_aps[i]
                    rhs = rhs_fn(i, h)
                    out = ps[k][:, h * 512:(h + 1) * 512]
                    is_last = (i == n - 1 and h == 1)

                    def fn(e, out=out, lhs=lhs, rhs=rhs, st=(i == 0), sp=(i == n - 1)):
                        return e.matmul(out, lhs, rhs, start=st, stop=sp)
                    if is_last:
                        ev = plan.add("tensor", fn, waits=waits, inc=s_pe)
                    else:
                        plan.add("tensor", fn, waits=waits)
            return ev

        def dma_to(dst, src):
            def fn(e, dst=dst, src=src):
                return e.dma_start(out=dst, in_=src)
            return fn

        consts_evt = None
        for dst, src in (
            (ones_sb[:, :], ones_d), (triu_sb[:, :], triu_d),
            (s1_sb[:, :], s1c), (s2_sb[:, :], s2c),
            (sh1_sb[:, :], sh1c), (sh2_sb[:, :], sh2c),
            (s3_sb[:, :], s3c), (bias_sb[:, :], biasc),
        ):
            consts_evt = plan.add("scalar", dma_to(dst, src), inc=s_cst, inc_n=16)

        plan.add("vector", lambda e: e.memset(cst[:, 0:1], EPS))
        cst_evt = plan.add("vector", lambda e: e.memset(cst[:, 1:2], -0.5), inc=s_ve)

        def warm_ln_table():
            plan.add("scalar",
                     lambda e: e.activation(junk[:, 0:1], cst[:, 0:1], AF.Ln,
                                            bias=cst[:, 0:1]),
                     waits=[cst_evt])

        warm_ln_table()

        stat_m = stat[:, 0:C]
        stat_msq = stat[:, C:2 * C]
        stat_r = stat[:, 2 * C:3 * C]

        def emit_acc_add(acc_ap, src_tile, i, waits):
            if i == 0:
                def f(e, a=acc_ap, s=src_tile):
                    return e.tensor_copy(a, s)
            else:
                def f(e, a=acc_ap, s=src_tile):
                    return e.tensor_add(a, a, s)
            return plan.add("vector", f, waits=waits, inc=s_ve)

        def ln_phase(x_ready, war_xbf_x2, war_xn, war_tmp, war_ha,
                     pool=(0, 1, 2, 3), make_xn=True, precast=None):
            aS_evt = aSS_evt = None
            if precast is not None:
                if len(precast) == 4:
                    xbf_evt, x2_evt, aS_evt, aSS_evt = precast
                else:
                    xbf_evt, x2_evt = precast
            else:
                xbf_evt = [None] * ND
                x2_evt = [None] * ND
                for i in range(ND):
                    def fc(e, i=i):
                        return e.tensor_copy(vre(XBF, i), x_t(i))
                    xbf_evt[i] = plan.add("vector", fc,
                                          waits=x_ready(i) + [war_xbf_x2], inc=s_ve)
                    if not make_xn and i < ND - 1:
                        aS_evt = emit_acc_add(vre(XN, 1), vre(XBF, i), i,
                                              [xbf_evt[i], war_xn, aS_evt])
                for i in range(ND):
                    def fs(e, i=i):
                        return e.activation(vre(X2, i), x_t(i), AF.Square)
                    x2_evt[i] = plan.add("scalar", fs,
                                         waits=x_ready(i) + [war_xbf_x2], inc=s_sc)
                    if not make_xn and i < ND - 1:
                        aSS_evt = emit_acc_add(vre(XN, 2), vre(X2, i), i,
                                               [x2_evt[i], war_xn, aSS_evt])

            if not make_xn and aS_evt is not None:
                def stats_group(k, acc_ap, t7_off, a_evt, t7_evt, free_evt):
                    last = None
                    for hh in range(2):
                        w1_ = [free_evt, consts_evt, a_evt] if hh == 0 else []
                        def fa(e, k=k, hh=hh, a=acc_ap):
                            return e.matmul(ps[k][:, hh * 512:(hh + 1) * 512],
                                            ones_sb[:, 0:P],
                                            a[:, hh * 512:(hh + 1) * 512],
                                            start=True, stop=False)
                        plan.add("tensor", fa, waits=w1_)
                        w2_ = [t7_evt] if hh == 0 else []
                        def fb(e, k=k, hh=hh, t7_off=t7_off):
                            return e.matmul(ps[k][:, hh * 512:(hh + 1) * 512],
                                            ones_sb[:, 0:P],
                                            vre_h(t7_off, ND - 1, hh),
                                            start=False, stop=True)
                        if hh == 1:
                            last = plan.add("tensor", fb, waits=w2_, inc=s_pe)
                        else:
                            plan.add("tensor", fb, waits=w2_)
                    return last

                kS, freeS = psum_acquire(pool)
                evS = stats_group(kS, vre(XN, 1), XBF, aS_evt, xbf_evt[ND - 1], freeS)
                kSS, freeSS = psum_acquire(pool)
                evSS = stats_group(kSS, vre(XN, 2), X2, aSS_evt, x2_evt[ND - 1], freeSS)
            else:
                kS, freeS = psum_acquire(pool)
                evS = mm_group(kS, [ones_sb[:, 0:P]] * ND,
                               lambda i, h: vre_h(XBF, i, h),
                               first_waits=[freeS, consts_evt],
                               per_i_waits=lambda i: [xbf_evt[i]])
                kSS, freeSS = psum_acquire(pool)
                evSS = mm_group(kSS, [ones_sb[:, 0:P]] * ND,
                                lambda i, h: vre_h(X2, i, h),
                                first_waits=[freeSS, consts_evt],
                                per_i_waits=lambda i: [x2_evt[i]])

            m_ap, msq_ap, r_ap = stat_m, stat_msq, stat_r
            mbf_ap = vre(HA, 0)
            rbf_ap = vre(HA, 1)

            ev_m = plan.add("vector",
                            lambda e, kS=kS: e.tensor_scalar_mul(m_ap, ps[kS][:, 0:C], INV_D),
                            waits=[evS, (s_sc, s_sc.n)], inc=s_ve)
            ps_free[kS] = ev_m
            ev_msq = plan.add("scalar",
                              lambda e: e.activation(msq_ap, m_ap, AF.Square),
                              waits=[ev_m], inc=s_sc)
            mbf_evt = plan.add("vector", lambda e: e.tensor_copy(mbf_ap, m_ap),
                     waits=[war_ha, ev_m], inc=s_ve)
            sub_evt = [None] * ND
            n_pre = ND if not make_xn else 3

            def emit_subs(rng):
                for i in rng:
                    def fsub(e, i=i):
                        return e.tensor_sub(u_t(i), vre(XBF, i), mbf_ap)
                    sub_evt[i] = plan.add("vector", fsub,
                                          waits=[war_tmp, mbf_evt], inc=s_ve)

            emit_subs(range(n_pre))
            ev_var = plan.add("vector",
                              lambda e, kSS=kSS: e.scalar_tensor_tensor(
                                  r_ap, ps[kSS][:, 0:C], INV_D, msq_ap,
                                  ALU.mult, ALU.subtract),
                              waits=[evSS, ev_msq], inc=s_ve)
            ps_free[kSS] = ev_var
            ln_evt = plan.add("scalar",
                              lambda e: e.activation(r_ap, r_ap, AF.Ln, bias=cst[:, 0:1]),
                              waits=[ev_var, cst_evt], inc=s_sc)
            rstd_evt = plan.add("scalar",
                                lambda e: e.activation(r_ap, r_ap, AF.Exp, scale=cst[:, 1:2]),
                                waits=[ln_evt], inc=s_sc)
            emit_subs(range(n_pre, ND))
            if not make_xn:
                return sub_evt, rstd_evt, ev_msq
            def warm(gate, n=4, kS=None):
                for _ in range(n):
                    def fd(e, kS=kS):
                        return e.matmul(ps[kS][:, 0:128], ones_sb[:, 0:P],
                                        ones_sb[:, 0:P], start=True, stop=True)
                    plan.add("tensor", fd, waits=[gate])
            warm(ev_var, kS=kS)
            warm(rstd_evt, kS=kS)
            rbf_evt = plan.add("vector", lambda e: e.tensor_copy(rbf_ap, r_ap),
                                waits=[rstd_evt], inc=s_ve)
            warm(rbf_evt, kS=kS)
            xn_evt = [None] * ND
            for i in range(ND):
                def fmul(e, i=i):
                    return e.tensor_mul(vre(XN, i), u_t(i), rbf_ap)
                xn_evt[i] = plan.add("vector", fmul,
                                     waits=[war_xn, rbf_evt, sub_evt[i]], inc=s_ve)
            return xn_evt, rstd_evt, ev_msq

        store_done = None
        war_u = None
        war_ha_ve = None
        attn_end_pe = None
        w1w2_end_pe = None
        h1_end_pe = None
        h2_end_pe = None
        u_all_evt = None

        next_pre = None
        for cidx in range(NCH):
            c0 = cidx * C

            if next_pre is None:
                xl_evt = [None] * ND
                for i in range(ND):
                    def fxl(e, i=i, c0=c0):
                        return e.dma_start(out=x_t(i),
                                           in_=xT[i * P:(i + 1) * P, c0:c0 + C])
                    xl_evt[i] = plan.add("sync", fxl, waits=[],
                                         inc=s_xl[i], inc_n=16)
                pre_cast = None
            else:
                xl_evt = next_pre["xl"]
                pre_cast = (next_pre["xbf"], next_pre["x2"],
                            next_pre["accS"], next_pre["accSS"])
                next_pre = None

            attn_dma = [enqueue_wdma(wT, ND, 512, og) for og in range(2)]
            w12_dma = []
            for og in range(8):
                w12_dma.append((enqueue_wdma(w1T, ND, 512, og),
                                enqueue_wdma(w2T, ND, 512, og)))
            h1_dma = [enqueue_wdma(h1T, NH, 128, o) for o in range(NH)]
            h2_dma = [enqueue_wdma(h2T, NH, 128, o) for o in range(NH)]
            w3_dma = [enqueue_wdma(w3T, NH, 128, o) for o in range(ND)]
            pump_dmas()

            sub_evt, rstd1_evt, msq1_evt = ln_phase(
                lambda i: [xl_evt[i]],
                war_xbf_x2=h2_end_pe, war_xn=w1w2_end_pe,
                war_tmp=war_u, war_ha=war_u, pool=(2, 3), make_xn=False,
                precast=pre_cast)

            ha_evt = [None] * ND
            xadd_evt = [None] * ND
            tt_evt = [None] * ND
            xbf2_evt = [None] * ND
            x22_evt = [None] * ND
            sbf_state = [None]

            def emit_attn(o):
                og = o // 4
                j = o % 4
                idx = attn_dma[og]
                q = slot_of(idx)
                blocks = [wb[:, q * 4096 + i * 512 + j * P: q * 4096 + i * 512 + j * P + P]
                          for i in range(ND)]
                kH, freeH = psum_acquire()
                evH = mm_group(kH, blocks, lambda i, h: u_th(i, h),
                               first_waits=[freeH, dma_ready[idx]],
                               per_i_waits=lambda i: [sub_evt[i]])
                if j == 3:
                    dma_mm_done[idx] = (s_pe, s_pe.n)
                    pump_dmas()

                scr = stat_m if o % 2 == 0 else stat_msq
                scr_war = [msq1_evt] if o < 2 else [ha_evt[o - 2]]
                def ftt(e, kH=kH, scr=scr):
                    return e.tensor_tensor(scr, ps[kH][:, 0:C], stat_r, ALU.mult)
                tt_evt[o] = plan.add("vector", ftt,
                                     waits=[evH, rstd1_evt] + scr_war, inc=s_ve)
                ps_free[kH] = tt_evt[o]

                ha_war = [war_ha_ve] if (war_ha_ve and o < 2) else []
                def fha(e, o=o, scr=scr):
                    return e.activation(vre(HA, o), scr, AF.Identity,
                                        bias=bias_sb[:, o:o + 1])
                ha_evt[o] = plan.add("scalar", fha,
                                     waits=[tt_evt[o], consts_evt] + ha_war, inc=s_sc)

            def emit_y(o):
                kY, freeY = psum_acquire()
                evY = None
                for hh in range(2):
                    waits = [freeY, consts_evt, ha_evt[o]] if hh == 0 else []
                    def f1(e, kY=kY, o=o, hh=hh):
                        return e.matmul(ps[kY][:, hh * 512:(hh + 1) * 512],
                                        triu_sb[:, 0:P], vre_h(HA, o, hh),
                                        start=True, stop=(o == 0))
                    if o == 0 and hh == 1:
                        evY = plan.add("tensor", f1, waits=waits, inc=s_pe)
                    else:
                        plan.add("tensor", f1, waits=waits)
                if o > 0:
                    for hh in range(2):
                        waits = [sbf_state[0]] if hh == 0 else []
                        def f2(e, kY=kY, hh=hh):
                            return e.matmul(ps[kY][:, hh * 512:(hh + 1) * 512],
                                            ones_sb[:, 0:P],
                                            v_sb[:, XN + hh * 512: XN + (hh + 1) * 512],
                                            start=False, stop=True)
                        if hh == 1:
                            evY = plan.add("tensor", f2, waits=waits, inc=s_pe)
                        else:
                            plan.add("tensor", f2, waits=waits)
                def fadd(e, o=o, kY=kY):
                    return e.tensor_add(x_t(o), x_t(o), ps[kY][:, 0:C])
                xadd_evt[o] = plan.add("vector", fadd,
                                       waits=[evY, xl_evt[o]], inc=s_ve)
                ps_free[kY] = xadd_evt[o]

                if o == 0:
                    sbf_state[0] = plan.add(
                        "vector",
                        lambda e: e.tensor_copy(vre(XN, 0), vre(HA, 0)),
                        waits=[ha_evt[0]] + ([w1w2_end_pe] if w1w2_end_pe else []),
                        inc=s_ve)
                elif o < ND - 1:
                    def fsu(e, o=o):
                        return e.tensor_add(vre(XN, 0), vre(XN, 0), vre(HA, o))
                    sbf_state[0] = plan.add("vector", fsu,
                                            waits=[ha_evt[o], evY, sbf_state[0]],
                                            inc=s_ve)

                def fc2(e, o=o):
                    return e.tensor_copy(vre(XBF, o), x_t(o))
                xbf2_evt[o] = plan.add("vector", fc2, waits=[xadd_evt[o]],
                                       inc=s_ve)
                def fs2(e, o=o):
                    return e.activation(vre(X2, o), x_t(o), AF.Square)
                x22_evt[o] = plan.add("scalar", fs2, waits=[xadd_evt[o]],
                                      inc=s_sc)

            for o in range(ND):
                emit_attn(o)
                if o >= 1:
                    emit_y(o - 1)
            emit_y(ND - 1)
            attn_end_pe = (s_pe, s_pe.n)

            xn2_evt, rstd2_evt, msq2_evt = ln_phase(lambda i: [xadd_evt[i]],
                                     war_xbf_x2=attn_end_pe,
                                     war_xn=attn_end_pe,
                                     war_tmp=attn_end_pe, war_ha=attn_end_pe,
                                     precast=(xbf2_evt, x22_evt))
            ln2_stats_end = (s_pe, s_pe.n)
            ln2_ve_end = (s_ve, s_ve.n)
            ln2_sc_end = (s_sc, s_sc.n)

            u_evt = [None] * NH
            for og in range(8):
                idx1, idx2 = w12_dma[og]
                q1, q2 = slot_of(idx1), slot_of(idx2)
                for j in range(4):
                    o = og * 4 + j
                    b1 = [wb[:, q1 * 4096 + i * 512 + j * P: q1 * 4096 + i * 512 + j * P + P]
                          for i in range(ND)]
                    b2 = [wb[:, q2 * 4096 + i * 512 + j * P: q2 * 4096 + i * 512 + j * P + P]
                          for i in range(ND)]
                    kA, freeA = psum_acquire()
                    evA = mm_group(kA, b1, lambda i, h: vre_h(XN, i, h),
                                   first_waits=[freeA, dma_ready[idx1]],
                                   per_i_waits=lambda i: [xn2_evt[i]])
                    kB, freeB = psum_acquire()
                    evB = mm_group(kB, b2, lambda i, h: vre_h(XN, i, h),
                                   first_waits=[freeB, dma_ready[idx2]])

                    at = o % 2
                    if o >= 2:
                        a_war = [u_evt[o - 2]]
                    else:
                        a_war = [attn_end_pe, war_ha_ve]

                    def fsilu(e, at=at, kA=kA, o=o):
                        return e.activation(vre(HA, at), ps[kA][:, 0:C], AF.Silu,
                                            scale=s1_sb[:, o:o + 1])
                    a_evt = plan.add("scalar", fsilu, waits=[evA] + a_war, inc=s_sc)
                    ps_free[kA] = a_evt

                    u_war = [war_u] if (war_u and o < 1) else []
                    def fu(e, o=o, kB=kB, at=at):
                        return e.scalar_tensor_tensor(u_t(o), ps[kB][:, 0:C],
                                                      s2_sb[:, o:o + 1], vre(HA, at),
                                                      ALU.mult, ALU.mult)
                    u_evt[o] = plan.add("vector", fu, waits=[evB, a_evt] + u_war, inc=s_ve)
                    ps_free[kB] = u_evt[o]
                dma_mm_done[idx1] = dma_mm_done[idx2] = (s_pe, s_pe.n)
                pump_dmas()
            w1w2_end_pe = (s_pe, s_pe.n)
            u_all_evt = u_evt[NH - 1]
            war_ha_ve = u_evt[NH - 1]

            v_evt = [None] * NH
            for o in range(NH):
                idx = h1_dma[o]
                q = slot_of(idx)
                blocks = [wb[:, q * 4096 + i * P: q * 4096 + (i + 1) * P]
                          for i in range(NH)]
                kP, freeP = psum_acquire()
                ev = mm_group(kP, blocks, lambda i, h: u_th(i, h),
                              first_waits=[freeP, dma_ready[idx]],
                              per_i_waits=lambda i: [u_evt[i]])
                dma_mm_done[idx] = (s_pe, s_pe.n)
                pump_dmas()
                v_war = [w1w2_end_pe, war_ha_ve] if o < 1 else []
                def fg(e, o=o, kP=kP):
                    return e.activation(v_t(o), ps[kP][:, 0:C], AF.Gelu,
                                        scale=sh1_sb[:, o:o + 1])
                v_evt[o] = plan.add("scalar", fg, waits=[ev] + v_war, inc=s_sc)
                ps_free[kP] = v_evt[o]
            h1_end_pe = (s_pe, s_pe.n)

            w_evt = [None] * NH
            for o in range(NH):
                idx = h2_dma[o]
                q = slot_of(idx)
                blocks = [wb[:, q * 4096 + i * P: q * 4096 + (i + 1) * P]
                          for i in range(NH)]
                kP, freeP = psum_acquire()
                ev = mm_group(kP, blocks, lambda i, h: v_th(i, h),
                              first_waits=[freeP, dma_ready[idx]],
                              per_i_waits=lambda i: [v_evt[i]])
                dma_mm_done[idx] = (s_pe, s_pe.n)
                pump_dmas()
                w_war = [h1_end_pe] if o < 1 else []
                def fg2(e, o=o, kP=kP):
                    return e.activation(u_t(o), ps[kP][:, 0:C], AF.Gelu,
                                        scale=sh2_sb[:, o:o + 1])
                w_evt[o] = plan.add("scalar", fg2, waits=[ev] + w_war, inc=s_sc)
                ps_free[kP] = w_evt[o]
            h2_end_pe = (s_pe, s_pe.n)

            xout_evt = [None] * ND
            st_evt = [None] * ND
            if cidx + 1 < NCH:
                warm_ln_table()
            ex_evt = [None, None]
            if cidx + 1 < NCH:
                next_pre = {"xl": [None] * ND, "xbf": [None] * ND,
                            "x2": [None] * ND, "accS": None, "accSS": None}
                c1e = (cidx + 1) * C
                for t_, dst_ in ((6, stat_m), (7, stat_msq)):
                    def fex(e, t_=t_, dst_=dst_, c1e=c1e):
                        return e.dma_start(out=dst_,
                                           in_=xT[t_ * P:(t_ + 1) * P, c1e:c1e + C])
                    ex_evt[t_ - 6] = plan.add("scalar", fex,
                                              waits=[ln2_ve_end, ln2_sc_end],
                                              inc=s_ex[t_ - 6], inc_n=16)
            for o in range(ND):
                idx = w3_dma[o]
                q = slot_of(idx)
                blocks = [wb[:, q * 4096 + i * P: q * 4096 + (i + 1) * P]
                          for i in range(NH)]
                kP, freeP = psum_acquire(pool=(0, 1))
                ev = mm_group(kP, blocks, lambda i, h: u_th(i, h),
                              first_waits=[freeP, dma_ready[idx]],
                              per_i_waits=lambda i: [w_evt[i]])
                dma_mm_done[idx] = (s_pe, s_pe.n)
                pump_dmas()
                def fw3(e, o=o, kP=kP):
                    return e.scalar_tensor_tensor(x_t(o), ps[kP][:, 0:C],
                                                  s3_sb[:, o:o + 1], x_t(o),
                                                  ALU.mult, ALU.add)
                xout_evt[o] = plan.add("vector", fw3, waits=[ev], inc=s_ve)
                ps_free[kP] = xout_evt[o]

                def fst(e, o=o, c0=c0):
                    return e.dma_start(out=outT[o * P:(o + 1) * P, c0:c0 + C],
                                       in_=x_t(o))
                st_evt[o] = plan.add("sync", fst, waits=[xout_evt[o]],
                                     inc=s_st[o], inc_n=16)

                if cidx + 1 < NCH:
                    c1 = (cidx + 1) * C
                    def fxl2(e, o=o, c1=c1):
                        return e.dma_start(out=x_t(o),
                                           in_=xT[o * P:(o + 1) * P, c1:c1 + C])
                    next_pre["xl"][o] = plan.add("sync", fxl2,
                                                 waits=[st_evt[o]],
                                                 inc=s_xl[o], inc_n=16)
                    xsrc = x_t(o) if o < 6 else (stat_m if o == 6 else stat_msq)
                    src_rdy = ([next_pre["xl"][o]] if o < 6 else [ex_evt[o - 6]])
                    def fc3(e, o=o, xsrc=xsrc):
                        return e.tensor_copy(vre(XBF, o), xsrc)
                    next_pre["xbf"][o] = plan.add(
                        "vector", fc3,
                        waits=src_rdy + [ln2_stats_end], inc=s_ve)
                    def fs3(e, o=o, xsrc=xsrc):
                        return e.activation(vre(X2, o), xsrc, AF.Square)
                    next_pre["x2"][o] = plan.add(
                        "scalar", fs3,
                        waits=src_rdy + [ln2_stats_end], inc=s_sc)
                    if o < ND - 1:
                        next_pre["accS"] = emit_acc_add(
                            vre(XN, 1), vre(XBF, o), o,
                            [next_pre["xbf"][o], next_pre["accS"],
                             w1w2_end_pe])
                        next_pre["accSS"] = emit_acc_add(
                            vre(XN, 2), vre(X2, o), o,
                            [next_pre["x2"][o], next_pre["accSS"],
                             w1w2_end_pe])
            war_u = (s_pe, s_pe.n)
            store_done = list(st_evt)

        plan.add("sync", None, waits=list(store_done))

        def emitter(eng_name):
            def run(e):
                waited = {}
                for fn, waits, inc, inc_n in plan.ops[eng_name]:
                    for (s, v) in waits:
                        key = id(s)
                        if waited.get(key, 0) < v:
                            e.wait_ge(s.h, v)
                            waited[key] = v
                    if fn is None:
                        continue
                    inst = fn(e)
                    if inc is not None:
                        inst.then_inc(inc.h, inc_n)
            return run

        block.sync(emitter("sync"))
        block.gpsimd(emitter("gpsimd"))
        block.tensor(emitter("tensor"))
        block.vector(emitter("vector"))
        block.scalar(emitter("scalar"))

    return nc


_NC_CACHE = None


def kernel(L0_attn_x, L0_attn_W, L0_attn_bias,
           L0_ffn_w1_w, L0_ffn_w1_s, L0_ffn_w2_w, L0_ffn_w2_s,
           L0_ffn_h1_w, L0_ffn_h1_s, L0_ffn_h2_w, L0_ffn_h2_s,
           L0_ffn_w3_w, L0_ffn_w3_s):
    global _NC_CACHE
    n_cores = 8
    asc = np.ascontiguousarray

    x = np.asarray(L0_attn_x, dtype=np.float32)
    xT_full = asc(x.reshape(16384, 1024).T)

    wT = asc(np.asarray(L0_attn_W, np.float32).T)
    w1T = asc(np.asarray(L0_ffn_w1_w, np.float32).T)
    w2T = asc(np.asarray(L0_ffn_w2_w, np.float32).T)
    h1T = asc(np.asarray(L0_ffn_h1_w, np.float32).T)
    h2T = asc(np.asarray(L0_ffn_h2_w, np.float32).T)
    w3T = asc(np.asarray(L0_ffn_w3_w, np.float32).T)

    def col(s, n):
        return asc(np.asarray(s, np.float32).reshape(n, P).T)

    s1c = col(L0_ffn_w1_s, 32)
    s2c = col(L0_ffn_w2_s, 32)
    sh1c = col(L0_ffn_h1_s, 32)
    sh2c = col(L0_ffn_h2_s, 32)
    s3c = col(L0_ffn_w3_s, 8)
    biasc = col(L0_attn_bias, 8)

    ones = np.ones((P, P), dtype=BF16NP)
    triu = np.triu(np.ones((P, P), np.float32)).astype(BF16NP)

    if _NC_CACHE is None:
        _NC_CACHE = build_nc()
    nc = _NC_CACHE

    shared = dict(wT=wT, w1T=w1T, w2T=w2T, h1T=h1T, h2T=h2T, w3T=w3T,
                  s1c=s1c, s2c=s2c, sh1c=sh1c, sh2c=sh2c, s3c=s3c,
                  biasc=biasc, ones=ones, triu=triu)
    in_maps = []
    for k in range(n_cores):
        m = dict(shared)
        m["xT"] = asc(xT_full[:, k * TPC:(k + 1) * TPC])
        in_maps.append(m)

    res = run_bass_kernel_spmd(nc, in_maps, core_ids=list(range(n_cores)))
    outT = np.concatenate([np.asarray(res.results[k]["outT"]) for k in range(n_cores)],
                          axis=1)
    return asc(outT.T).reshape(8, 2048, 1024).astype(np.float32, copy=False)
